# version 08
# baseline (speedup 1.0000x reference)
import numpy as np
import concourse.bass as bass
import concourse.mybir as mybir
from concourse.bass_utils import run_bass_kernel_spmd
from contextlib import ExitStack

F32 = mybir.dt.float32
BF16 = mybir.dt.bfloat16
AF = mybir.ActivationFunctionType
ALU = mybir.AluOpType

SEQ, S, D, NH, DH = 2, 2048, 1024, 16, 4096
DC, HC, KC, QCH = D // 128, DH // 128, S // 128, S // 512
NP, TP = 4, 1024
EXPS = -20.0
EPS = 1e-5
NCORES = 8


def build():
    nc = bass.Bass("TRN2", target_bir_lowering=False)
    dt = nc.dram_tensor
    xT = dt("xT", [SEQ, DC, 128, S], F32, kind="ExternalInput").ap()
    xr = dt("xr", [SEQ, S, D], F32, kind="ExternalInput").ap()
    wq = dt("wq", [128, DC, D], F32, kind="ExternalInput").ap()
    wk = dt("wk", [128, DC, D], F32, kind="ExternalInput").ap()
    wv = dt("wv", [128, DC, D], F32, kind="ExternalInput").ap()
    wo = dt("wo", [128, DC, D], F32, kind="ExternalInput").ap()
    bqp = dt("bqp", [128, DC], F32, kind="ExternalInput").ap()
    bkp = dt("bkp", [128, DC], F32, kind="ExternalInput").ap()
    bvb = dt("bvb", [128, D], F32, kind="ExternalInput").ap()
    bob = dt("bob", [128, D], F32, kind="ExternalInput").ap()
    w1 = dt("w1", [HC, 128, DC * 128], F32, kind="ExternalInput").ap()
    w2 = dt("w2", [HC, 128, DC * 128], F32, kind="ExternalInput").ap()
    h1 = dt("h1", [HC, 128, HC * 128], F32, kind="ExternalInput").ap()
    h2 = dt("h2", [HC, 128, HC * 128], F32, kind="ExternalInput").ap()
    w3 = dt("w3", [HC, 128, D], F32, kind="ExternalInput").ap()
    s12 = dt("s12", [128, 2 * HC], F32, kind="ExternalInput").ap()
    sh12 = dt("sh12", [128, 2 * HC], F32, kind="ExternalInput").ap()
    s3b = dt("s3b", [128, D], F32, kind="ExternalInput").ap()
    idn = dt("idn", [128, 128], F32, kind="ExternalInput").ap()
    out = dt("out", [SEQ, S, D], F32, kind="ExternalOutput").ap()

    w1b = dt("w1b", [HC, 128, DC * 128], BF16, kind="Internal").ap()
    w2b = dt("w2b", [HC, 128, DC * 128], BF16, kind="Internal").ap()
    h1b = dt("h1b", [HC, 128, HC * 128], BF16, kind="Internal").ap()
    h2b = dt("h2b", [HC, 128, HC * 128], BF16, kind="Internal").ap()
    w3b = dt("w3b", [HC, 128, D], BF16, kind="Internal").ap()
    wqkvb = dt("wqkvb", [3, 128, DC * D], BF16, kind="Internal").ap()
    xT1b = dt("xT1b", [DC * 128, S], BF16, kind="Internal").ap()
    yf = dt("yf", [SEQ * S, D], F32, kind="Internal").ap()
    yb = dt("yb", [SEQ * S, D], BF16, kind="Internal").ap()

    ctx = ExitStack()
    sb = lambda n_, s_, d_: ctx.enter_context(nc.sbuf_tensor(n_, s_, d_))
    cst = sb("cst", [128, 8], F32)
    ones = sb("ones", [1, 512], BF16)
    bq_sb = sb("bq_sb", [128, DC], F32)
    bk_sb = sb("bk_sb", [128, DC], F32)
    s12_sb = sb("s12_sb", [128, 2 * HC], F32)
    sh12_sb = sb("sh12_sb", [128, 2 * HC], F32)
    s3b_sb = sb("s3b_sb", [128, D], F32)

    names = ["dSm", "dXT", "dWA", "dRL", "dC", "dO", "pe", "ex", "dv", "ac2",
             "vself", "aself",
             "dY00", "dY01", "dY10", "dY11", "dXR0", "dXR1",
             "dWB0", "dWB1", "dYT", "dYF0", "dYF1"]
    H = {x: ctx.enter_context(nc.semaphore(x)) for x in names}
    n = {x: 0 for x in names}

    def inc(name, ins, k=1):
        ins.then_inc(H[name], k)
        n[name] += k
        return n[name]

    def vflush(v, ins):
        v.wait_ge(H["vself"], inc("vself", ins))

    def aflush(sc, ins):
        sc.wait_ge(H["aself"], inc("aself", ins))

    block = ctx.enter_context(nc.Block())
    marks = {}
    ystore = [None] * (SEQ * 16)

    att = ExitStack()
    sA = lambda n_, s_, d_: att.enter_context(nc.sbuf_tensor(n_, s_, d_))
    woT = sA("woT", [128, DC * D], BF16)
    identb = sA("identb", [128, 128], F32)
    bvb_sb = sA("bvb_sb", [128, D], F32)
    bob_sb = sA("bob_sb", [128, D], F32)
    QT = sA("QT", [128, DC * S], BF16)
    KT = sA("KT", [128, DC * S], BF16)
    Vg = sA("Vg", [128, KC * NH * 65], BF16)
    Vg4 = Vg.rearrange("p (j h e) -> p j h e", j=KC, h=NH)

    attps = ExitStack()
    pS = [attps.enter_context(nc.psum_tensor(f"pS{i}", [128, 1024], F32)) for i in range(2)]
    pV = [attps.enter_context(nc.psum_tensor(f"pV{i}", [128, 512], F32)) for i in range(2)]
    pAV = [attps.enter_context(nc.psum_tensor(f"pAV{i}", [128, 260], F32)) for i in range(2)]

    def cast_chunks(g, dst, src, tag):
        cc, pp, ff = src.shape
        fi = src.rearrange("c p f -> (c p) f")
        fo = dst.rearrange("c p f -> (c p) f")
        step = max(1, (8 << 20) // (ff * 4))
        r = 0
        while r < cc * pp:
            e = min(cc * pp, r + step)
            inc("dC", g.dma_start(out=fo[r:e, :], in_=fi[r:e, :]), 16)
            r = e
        marks[tag] = n["dC"]

    @block.gpsimd
    def _(g):
        for d_, s_ in [(bq_sb, bqp), (bk_sb, bkp), (bvb_sb, bvb), (bob_sb, bob),
                       (s12_sb, s12), (sh12_sb, sh12), (s3b_sb, s3b), (identb, idn)]:
            inc("dSm", g.dma_start(out=d_[:, :], in_=s_), 16)
        marks["small"] = n["dSm"]

    @block.vector
    def _(v):
        v.memset(cst[:, 0:1], 0.125)
        v.memset(cst[:, 1:2], EXPS)
        v.memset(cst[:, 2:3], EPS)
        v.memset(cst[:, 3:4], -0.5)
        v.memset(ones[:, :], 1.0)
        inc("dv", v.memset(Vg4[:, :, :, 64:65], 1.0))

    B_dv0, B_ex0 = {}, {}

    def headev(s, qc, h):
        return B_dv0[s] + qc * 29 + h + 1

    def attTev(s, qc, qt):
        return B_dv0[s] + qc * 29 + 16 + 2 * qt + 1

    def res1ev(s, qc, qt):
        return B_dv0[s] + qc * 29 + 16 + 2 * qt + 2

    def varev(s, qc):
        return B_dv0[s] + qc * 29 + 25

    def ynorm(s, qc, qt):
        return B_dv0[s] + qc * 29 + 25 + qt + 1

    def exv(s, qc, hp, j):
        return B_ex0[s] + (qc * 8 + hp) * KC + j + 1

    for s in range(SEQ):
        pha = ExitStack()
        XTs = pha.enter_context(nc.sbuf_tensor(f"XTs{s}", [128, DC * S], BF16))
        wbuf = [pha.enter_context(nc.sbuf_tensor(f"wbuf{s}_{i}", [128, DC * D], BF16))
                for i in range(2)]

        if s == 1:
            marks["wq1"] = 32
            marks["wk1"] = 32
            marks["wv1"] = 48
            marks["xT1"] = n["dXT"] + 16

        @block.gpsimd
        def _(g, s=s, XTs=XTs, wbuf=wbuf):
            if s == 0:
                for k in range(DC):
                    inc("dXT", g.dma_start(out=XTs[:, k * S:(k + 1) * S], in_=xT[0, k]), 16)
                marks["xT0"] = n["dXT"]
                inc("dWA", g.dma_start(out=wbuf[0][:, :], in_=wq.rearrange("p k o -> p (k o)")), 16)
                inc("dWA", g.dma_start(out=wbuf[1][:, :], in_=wk.rearrange("p k o -> p (k o)")), 16)
                inc("dWA", g.dma_start(out=woT[:, :], in_=wo.rearrange("p k o -> p (k o)")), 16)
                marks["wqkwo"] = n["dWA"]
                g.wait_ge(H["dWA"], marks["wqkwo"])
                inc("dWA", g.dma_start(out=wqkvb[0], in_=wbuf[0][:, :]), 16)
                inc("dWA", g.dma_start(out=wqkvb[1], in_=wbuf[1][:, :]), 16)
                marks["wqkv_st12"] = n["dWA"]
                cast_chunks(g, w1b, w1, "w1b")
                cast_chunks(g, w2b, w2, "w2b")
                inc("dC", g.dma_start(out=xT1b,
                                      in_=xT[1].rearrange("k p t -> (k p) t")), 16)
                marks["xT1b"] = n["dC"]
            else:
                pass

        dvA0 = n["dv"]
        if s == 0:
            marks["wv0"] = n["dWA"] + 16

        @block.tensor
        def _(t, s=s, XTs=XTs, wbuf=wbuf):
            t.wait_ge(H["dXT"], marks["xT0"] if s == 0 else marks["xT1"])
            if s == 0:
                t.wait_ge(H["dSm"], marks["small"])
                t.wait_ge(H["dWA"], marks["wqkwo"])
            else:
                t.wait_ge(H["dRL"], 32)
            for w_i in range(2):
                for oc in range(DC):
                    for t4 in range(4):
                        i = w_i * 32 + oc * 4 + t4
                        if i >= 2:
                            t.wait_ge(H["dv"], dvA0 + (i - 2) + 1)
                        b = pS[i % 2][:, 0:512]
                        for k in range(DC):
                            ins = t.matmul(b,
                                           wbuf[w_i][:, k * D + oc * 128:k * D + oc * 128 + 128],
                                           XTs[:, k * S + t4 * 512:k * S + t4 * 512 + 512],
                                           start=(k == 0), stop=(k == DC - 1))
                        marks[("qk", s, i)] = inc("pe", ins)
            t.wait_ge(H["dWA"] if s == 0 else H["dRL"],
                      marks["wv0"] if s == 0 else marks["wv1"])
            for jj in range(KC):
                if jj >= 1:
                    t.wait_ge(H["dv"], dvA0 + 64 + (jj - 1) + 1)
                b0, b1 = pV[0], pV[1]
                for k in range(DC):
                    t.matmul(b0[:, :], XTs[:, k * S + jj * 128:k * S + jj * 128 + 128],
                             wbuf[0][:, k * D:k * D + 512], start=(k == 0), stop=(k == DC - 1))
                    ins = t.matmul(b1[:, :], XTs[:, k * S + jj * 128:k * S + jj * 128 + 128],
                                   wbuf[0][:, k * D + 512:k * D + 1024],
                                   start=(k == 0), stop=(k == DC - 1))
                marks[("v", s, jj)] = inc("pe", ins)
            marks[f"Vmm{s}"] = n["pe"]

        if s == 0:
            @block.gpsimd
            def _(g, wbuf=wbuf):
                g.wait_ge(H["pe"], marks[("qk", 0, 31)])
                g.wait_ge(H["dWA"], marks["wqkv_st12"])
                assert inc("dWA", g.dma_start(out=wbuf[0][:, :],
                                              in_=wv.rearrange("p k o -> p (k o)")), 16) == marks["wv0"]
                g.wait_ge(H["dWA"], marks["wv0"])
                inc("dWA", g.dma_start(out=wqkvb[2], in_=wbuf[0][:, :]), 16)
                marks["wqkv_st"] = n["dWA"]
                cast_chunks(g, h1b, h1, "h1b")
                cast_chunks(g, h2b, h2, "h2b")
                cast_chunks(g, w3b, w3, "w3b")
        else:
            @block.sync
            def _(sy, wbuf=wbuf, XTs=XTs):
                sy.wait_ge(H["dWA"], marks["wqkv_st"])
                sy.wait_ge(H["dC"], marks["xT1b"])
                sy.wait_ge(H["ex"], marks["B0_ex_end"])
                sy.wait_ge(H["pe"], marks[("tr", 0, QCH - 1, 3)])
                sy.wait_ge(H["dv"], marks["B0_hd_end"])
                assert inc("dXT", sy.dma_start(
                    out=XTs[:, :].rearrange("p (k t) -> p k t", k=DC),
                    in_=xT1b.rearrange("(k p) t -> p k t", p=128)), 16) == marks["xT1"]
                sy.wait_ge(H["dv"], marks["B0_dv_end"])
                inc("dRL", sy.dma_start(out=wbuf[0][:, :], in_=wqkvb[0]), 16)
                inc("dRL", sy.dma_start(out=wbuf[1][:, :], in_=wqkvb[1]), 16)
                sy.wait_ge(H["pe"], marks[("qk", 1, 31)])
                assert inc("dRL", sy.dma_start(out=wbuf[0][:, :], in_=wqkvb[2]), 16) == marks["wv1"]

        @block.vector
        def _(v, s=s):
            for w_i in range(2):
                dst = QT if w_i == 0 else KT
                bia = bq_sb if w_i == 0 else bk_sb
                for oc in range(DC):
                    for t4 in range(4):
                        i = w_i * 32 + oc * 4 + t4
                        v.wait_ge(H["pe"], marks[("qk", s, i)])
                        ins = v.tensor_single_scalar(
                            dst[:, oc * S + t4 * 512:oc * S + t4 * 512 + 512],
                            pS[i % 2][:, 0:512], bia[:, oc:oc + 1], ALU.add)
                        assert inc("dv", ins) == dvA0 + i + 1
            bvb3 = bvb_sb.rearrange("p (h e) -> p h e", e=64)
            for jj in range(KC):
                v.wait_ge(H["pe"], marks[("v", s, jj)])
                for half in range(2):
                    bb = pV[half]
                    base = jj * NH * 65 + half * 8 * 65
                    dst = Vg[:, base:base + 8 * 65].rearrange("p (h e) -> p h e", e=65)[:, :, 0:64]
                    ins = v.tensor_tensor(
                        dst,
                        bb[:, :].rearrange("p (h e) -> p h e", h=8),
                        bvb3[:, half * 8:half * 8 + 8, :],
                        ALU.add)
                assert inc("dv", ins) == dvA0 + 64 + jj + 1

        pha.close()

        phb = ExitStack()
        sB = lambda n_, s_, d_: phb.enter_context(nc.sbuf_tensor(n_, s_, d_))
        E = sB(f"E{s}", [128, 2 * 1024], BF16)
        att_o = [sB(f"att_o{s}_{i}", [128, 4 * D], F32) for i in range(2)]
        attT = sB(f"attT{s}", [128, 4 * D], BF16)
        r1 = sB(f"r1{s}", [128, 4 * D], F32)
        xrb = [sB(f"xrb{s}_{i}", [128, D], F32) for i in range(2)]
        yt = [sB(f"yt{s}_{i}", [128, D], F32) for i in range(2)]
        sqd = sB(f"sqd{s}", [128, D], F32)
        st1 = sB(f"st1{s}", [128, 32], F32)
        rz = sB(f"rz{s}", [128, 4], F32)

        B_dv0[s] = n["dv"]
        B_ex0[s] = n["ex"]
        aev = n["dv"]
        for qc_ in range(QCH):
            for qt_ in range(4):
                i_ = qc_ * 4 + qt_
                pn = "dXR%d" % (i_ % 2)
                marks[("xr", s, qc_, qt_)] = (pn, n[pn] + (i_ // 2 + 1) * 16)
        for gi in range(16):
            pn = "dY%d%d" % (s, gi % 2)
            ystore[s * 16 + gi] = (pn, n[pn] + (gi // 2 + 1) * 32)

        P = {}
        dvp = [n["dv"]]

        def assign(key):
            dvp[0] += 1
            P[key] = dvp[0]

        for qc_ in range(QCH):
            for hp_ in range(8):
                if qc_ >= 1:
                    if hp_ in (1, 3, 5, 7):
                        assign(("aT", qc_ - 1, (hp_ - 1) // 2))
                    elif hp_ in (2, 4, 6):
                        assign(("r1", qc_ - 1, hp_ // 2 - 1))
                assign(("hd", qc_, 2 * hp_))
                assign(("hd", qc_, 2 * hp_ + 1))
            if qc_ >= 1:
                assign(("r1", qc_ - 1, 3))
                assign(("var", qc_ - 1))
                for qt_ in range(4):
                    assign(("yn", qc_ - 1, qt_))
        ql = QCH - 1
        for qt_ in range(4):
            assign(("aT", ql, qt_))
            assign(("r1", ql, qt_))
        assign(("var", ql))
        for qt_ in range(4):
            assign(("yn", ql, qt_))

        def headev(s_, qc_, h_):
            return P[("hd", qc_, h_)]

        def attTev(s_, qc_, qt_):
            return P[("aT", qc_, qt_)]

        def res1ev(s_, qc_, qt_):
            return P[("r1", qc_, qt_)]

        def varev(s_, qc_):
            return P[("var", qc_)]

        def ynorm(s_, qc_, qt_):
            return P[("yn", qc_, qt_)]

        @block.tensor
        def _(t, s=s, E=E, att_o=att_o, attT=attT):
            t.wait_ge(H["dv"], aev)

            def spair(qc, hp, j):
                gg = hp * KC + j
                if gg >= 2:
                    t.wait_ge(H["ex"], exv(s, qc, (gg - 2) // KC, (gg - 2) % KC))
                elif qc > 0:
                    t.wait_ge(H["ex"], exv(s, qc - 1, 7, KC - 2 + gg))
                for hh in range(2):
                    h = 2 * hp + hh
                    dcx, pb = (h // 2) * S, 64 * hh
                    ins = t.matmul(pS[j % 2][:, hh * 512:hh * 512 + 512],
                                   KT[pb:pb + 64, dcx + j * 128:dcx + j * 128 + 128],
                                   QT[pb:pb + 64, dcx + qc * 512:dcx + qc * 512 + 512],
                                   start=True, stop=True)
                marks[("s", s, qc, hp, j)] = inc("pe", ins)

            def avpair(qc, hp, j):
                t.wait_ge(H["ex"], exv(s, qc, hp, j))
                for hh in range(2):
                    if j == 0:
                        ph = 2 * (hp - 1) + hh
                        if hp > 0:
                            t.wait_ge(H["dv"], headev(s, qc, ph))
                        elif qc > 0:
                            t.wait_ge(H["dv"], headev(s, qc - 1, NH - 2 + hh))
                    h = 2 * hp + hh
                    voff = j * NH * 65 + 65 * h
                    for qt in range(4):
                        ins = t.matmul(pAV[hh][:, qt * 65:qt * 65 + 65],
                                       E[:, (j % 2) * 1024 + hh * 512 + qt * 128:
                                          (j % 2) * 1024 + hh * 512 + qt * 128 + 128],
                                       Vg[:, voff:voff + 65],
                                       start=(j == 0 and qt == 0), stop=(j == KC - 1 and qt == 3))
                marks[("av", s, qc, hp, j)] = inc("pe", ins)

            def tr_step(qc2, qt):
                if qt == 0:
                    t.wait_ge(H["dv"], headev(s, qc2, NH - 1))
                    if qc2 > 0:
                        t.wait_ge(H["dv"], res1ev(s, qc2 - 1, 3))
                else:
                    t.wait_ge(H["dv"], res1ev(s, qc2, qt - 1))
                for k in range(DC):
                    ins = t.transpose(pV[k // 4][:, (k % 4) * 128:(k % 4) * 128 + 128],
                                      att_o[qc2 % 2][:, qt * D + k * 128:qt * D + k * 128 + 128],
                                      identb[:, :])
                marks[("tr", s, qc2, qt)] = inc("pe", ins)

            def wo_step(qc2, qt):
                t.wait_ge(H["dv"], attTev(s, qc2, qt))
                for k in range(DC):
                    t.matmul(pV[0][:, :], attT[:, qt * D + k * 128:qt * D + k * 128 + 128],
                             woT[:, k * D:k * D + 512], start=(k == 0), stop=(k == DC - 1))
                    ins = t.matmul(pV[1][:, :], attT[:, qt * D + k * 128:qt * D + k * 128 + 128],
                                   woT[:, k * D + 512:k * D + 1024],
                                   start=(k == 0), stop=(k == DC - 1))
                marks[("wom", s, qc2, qt)] = inc("pe", ins)

            for qc in range(QCH):
                for hp in range(8):
                    for j in range(KC):
                        spair(qc, hp, j)
                        if j > 0:
                            avpair(qc, hp, j - 1)
                    avpair(qc, hp, KC - 1)
                    if qc >= 1:
                        if hp % 2 == 0 and hp < 8:
                            tr_step(qc - 1, hp // 2)
                        else:
                            wo_step(qc - 1, (hp - 1) // 2)
            for qt in range(4):
                tr_step(QCH - 1, qt)
                wo_step(QCH - 1, qt)

        @block.scalar
        def _(sc, s=s, E=E):
            def ln1(qc2):
                sc.wait_ge(H["dv"], varev(s, qc2))
                aflush(sc, sc.activation(st1[:, 16:20], st1[:, 20:24], AF.Ln, bias=cst[:, 2:3]))
                ins = sc.activation(st1[:, 24:28], st1[:, 16:20], AF.Exp, scale=cst[:, 3:4])
                marks[("rstd", s, qc2)] = inc("ac2", ins)

            for qc in range(QCH):
                for hp in range(8):
                    for j in range(KC):
                        sc.wait_ge(H["pe"], marks[("s", s, qc, hp, j)])
                        gg = hp * KC + j
                        if gg >= 2:
                            sc.wait_ge(H["pe"], marks[("av", s, qc, (gg - 2) // KC, (gg - 2) % KC)])
                        elif qc > 0:
                            sc.wait_ge(H["pe"], marks[("av", s, qc - 1, 7, KC - 2 + gg)])
                        ins = sc.activation(E[:, (j % 2) * 1024:(j % 2) * 1024 + 1024],
                                            pS[j % 2][:, :],
                                            AF.Exp, bias=cst[:, 1:2], scale=cst[:, 0:1])
                        assert inc("ex", ins) == exv(s, qc, hp, j)
                if qc >= 1:
                    ln1(qc - 1)
            ln1(QCH - 1)

        @block.vector
        def _(v, s=s, E=E, att_o=att_o, attT=attT, xrb=xrb, yt=yt, sqd=sqd, st1=st1, rz=rz, r1=r1):
            if s == 0:
                v.wait_ge(H["dWA"], marks["wqkv_st"])

            def head_evac(qc2, h):
                v.wait_ge(H["pe"], marks[("av", s, qc2, h // 2, KC - 1)])
                if h > 0 or qc2 > 0 or s > 0:
                    v.wait_ge(H["dv"], n["dv"])
                for qt in range(4):
                    ins = v.reciprocal(rz[:, qt:qt + 1],
                                       pAV[h % 2][:, qt * 65 + 64:qt * 65 + 65])
                vflush(v, ins)
                for qt in range(4):
                    ins = v.tensor_single_scalar(
                        att_o[qc2 % 2][:, qt * D + 64 * h:qt * D + 64 * h + 64],
                        pAV[h % 2][:, qt * 65:qt * 65 + 64], rz[:, qt:qt + 1], ALU.mult)
                assert inc("dv", ins) == headev(s, qc2, h)

            def attT_evac(qc2, qt):
                v.wait_ge(H["pe"], marks[("tr", s, qc2, qt)])
                v.tensor_copy(attT[:, qt * D:qt * D + 512], pV[0][:, :])
                ins = v.tensor_copy(attT[:, qt * D + 512:qt * D + 1024], pV[1][:, :])
                assert inc("dv", ins) == attTev(s, qc2, qt)

            def res1_step(qc2, qt):
                xrm = marks[("xr", s, qc2, qt)]
                v.wait_ge(H[xrm[0]], xrm[1])
                vflush(v, v.tensor_tensor(xrb[qt % 2][:, :], xrb[qt % 2][:, :], bob_sb[:, :], ALU.add))
                v.wait_ge(H["pe"], marks[("wom", s, qc2, qt)])
                v.scalar_tensor_tensor(r1[:, qt * D:qt * D + 512], pV[0][:, :], 1.0,
                                       xrb[qt % 2][:, 0:512], ALU.mult, ALU.add,
                                       accum_out=st1[:, qt:qt + 1])
                ins = v.scalar_tensor_tensor(r1[:, qt * D + 512:qt * D + 1024], pV[1][:, :], 1.0,
                                             xrb[qt % 2][:, 512:1024], ALU.mult, ALU.add,
                                             accum_out=st1[:, 4 + qt:5 + qt])
                assert inc("dv", ins) == res1ev(s, qc2, qt)
                v.wait_ge(H["dv"], n["dv"])
                ins = v.scalar_tensor_tensor(sqd[:, :], r1[:, qt * D:qt * D + D], 1.0,
                                             r1[:, qt * D:qt * D + D], ALU.mult, ALU.mult,
                                             accum_out=st1[:, 8 + qt:9 + qt])
                vflush(v, ins)

            def ln1_post(qc2):
                vflush(v, v.tensor_tensor(st1[:, 12:16], st1[:, 0:4], st1[:, 4:8], ALU.add))
                vflush(v, v.tensor_single_scalar(st1[:, 12:16], st1[:, 12:16], 1.0 / D, ALU.mult))
                vflush(v, v.tensor_tensor(st1[:, 16:20], st1[:, 12:16], st1[:, 12:16], ALU.mult))
                vflush(v, v.tensor_single_scalar(st1[:, 20:24], st1[:, 8:12], 1.0 / D, ALU.mult))
                ins = v.tensor_tensor(st1[:, 20:24], st1[:, 20:24], st1[:, 16:20], ALU.subtract)
                assert inc("dv", ins) == varev(s, qc2)
                v.wait_ge(H["ac2"], marks[("rstd", s, qc2)])
                vflush(v, v.tensor_tensor(st1[:, 28:32], st1[:, 12:16], st1[:, 24:28], ALU.mult))
                for qt in range(4):
                    g = s * 16 + qc2 * 4 + qt
                    if g >= 2:
                        ym = ystore[g - 2]
                        v.wait_ge(H[ym[0]], ym[1])
                    ins = v.tensor_scalar(yt[qt % 2][:, :], r1[:, qt * D:qt * D + D],
                                          st1[:, 24 + qt:25 + qt], st1[:, 28 + qt:29 + qt],
                                          ALU.mult, ALU.subtract)
                    assert inc("dv", ins) == ynorm(s, qc2, qt)

            for qc in range(QCH):
                for hp in range(8):
                    if qc >= 1:
                        if hp in (1, 3, 5, 7):
                            attT_evac(qc - 1, (hp - 1) // 2)
                        elif hp in (2, 4, 6):
                            res1_step(qc - 1, hp // 2 - 1)
                    head_evac(qc, 2 * hp)
                    head_evac(qc, 2 * hp + 1)
                if qc >= 1:
                    res1_step(qc - 1, 3)
                    ln1_post(qc - 1)
            ql2 = QCH - 1
            for qt in range(4):
                attT_evac(ql2, qt)
                res1_step(ql2, qt)
            ln1_post(ql2)

        @block.sync
        def _(sy, s=s, xrb=xrb):
            if s == 0:
                sy.wait_ge(H["dWA"], marks["wqkv_st"])
            for qc in range(QCH):
                for qt in range(4):
                    i = qc * 4 + qt
                    if i >= 2:
                        sy.wait_ge(H["dv"], res1ev(s, (i - 2) // 4, (i - 2) % 4))
                    else:
                        sy.wait_ge(H["dv"], B_dv0[s])
                    r0 = qc * 512 + qt * 128
                    ins = sy.dma_start(out=xrb[qt % 2][:, :], in_=xr[s, r0:r0 + 128, :])
                    xrm = marks[("xr", s, qc, qt)]
                    assert inc(xrm[0], ins, 16) == xrm[1]

        @block.gpsimd
        def _(g, s=s, yt=yt):
            for qc in range(QCH):
                for qt in range(4):
                    gidx = s * 16 + qc * 4 + qt
                    r0 = s * S + qc * 512 + qt * 128
                    g.wait_ge(H["dv"], ynorm(s, qc, qt))
                    pn = ystore[gidx][0]
                    inc(pn, g.dma_start(out=yf[r0:r0 + 128, :], in_=yt[qt % 2][:, :]), 16)
                    inc(pn, g.dma_start(out=yb[r0:r0 + 128, :], in_=yt[qt % 2][:, :]), 16)
                    assert n[pn] == ystore[gidx][1]

        assert n["dv"] == dvp[0], (n["dv"], dvp[0])

        phb.close()
        if s == 0:
            marks["B0_dv_end"] = n["dv"]
            marks["B0_ex_end"] = n["ex"]
            marks["B0_hd_end"] = P[("hd", QCH - 1, NH - 1)]

    att_dv_done = n["dv"]
    att.close()
    attps.close()

    ffn = ExitStack()
    sF = lambda n_, s_, d_: ffn.enter_context(nc.sbuf_tensor(n_, s_, d_))
    Hb0 = sF("Hb0", [128, HC * TP], BF16)
    Hb1 = sF("Hb1", [128, HC * TP], BF16)
    yTs = sF("yTs", [128, DC * TP], BF16)
    wblk = [sF(f"wblk{i}", [128, HC * 128], BF16) for i in range(2)]
    r2 = sF("r2", [128, 4 * D], F32)
    yft = [sF(f"yft{i}", [128, D], F32) for i in range(2)]
    sqd2 = sF("sqd2", [128, D], F32)
    st2 = sF("st2", [128, 32], F32)
    ffps = ExitStack()
    pF = [ffps.enter_context(nc.psum_tensor(f"pF{i}", [128, 512], F32)) for i in range(8)]

    pe0 = n["pe"]
    dv0 = n["dv"]
    ac0 = n["ac2"]
    STG = [("w1b", w1b, DC), ("w2b", w2b, DC), ("h1b", h1b, HC), ("h2b", h2b, HC)]

    def peval(p, bi):
        return pe0 + p * 192 + bi + 1

    def ac2v(p, kind, m):
        base = ac0 + p * 194
        if kind == 3:
            return base + 192 + m + 1
        return base + kind * 64 + m + 1

    def dvv(p, kind, m):
        base = dv0 + p * 83
        if kind == 0:
            return base + m + 1
        if kind == 1:
            return base + 65
        sub, q4 = m // 4, m % 4
        if kind == 2:
            return base + 65 + sub * 9 + q4 + 1
        if kind == 3:
            return base + 65 + m * 9 + 5
        return base + 65 + sub * 9 + 5 + q4 + 1

    wld = {}
    yfl = {}
    outst = {}

    cast_total = n["dC"]

    @block.sync
    def _(sy):
        sy.wait_ge(H["dv"], att_dv_done)
        sy.wait_ge(H["dC"], cast_total)
        for yn_ in ("dY00", "dY01", "dY10", "dY11"):
            sy.wait_ge(H[yn_], 8 * 32)
        for p in range(NP):
            g0 = p * TP
            sq_ = p // 2
            sy.wait_ge(H["dY%d0" % sq_], 8 * 32)
            sy.wait_ge(H["dY%d1" % sq_], 8 * 32)
            if p > 0:
                sy.wait_ge(H["pe"], peval(p - 1, 63))
            for k in range(DC):
                ins = sy.dma_start(out=yTs[:, k * TP:(k + 1) * TP],
                                   in_=yb[g0:g0 + TP, k * 128:(k + 1) * 128], transpose=True)
                inc("dYT", ins, 16)
            marks[("yT", p)] = n["dYT"]
            for si, (tag, wsrc, kk) in enumerate(STG):
                for c in range(HC):
                    bi = si * 32 + c
                    gbi = p * 192 + bi
                    pn = "dWB%d" % (bi % 2)
                    if gbi >= 2:
                        pp, pbi = (gbi - 2) // 192, (gbi - 2) % 192
                        sy.wait_ge(H["pe"], peval(pp, pbi))
                    ins = sy.dma_start(out=wblk[c % 2][:, 0:kk * 128], in_=wsrc[c])
                    wld[(p, si, c)] = (pn, inc(pn, ins, 16))
            for sub in range(2):
                for k in range(HC):
                    bi = 128 + sub * 32 + k
                    gbi = p * 192 + bi
                    pn = "dWB%d" % (bi % 2)
                    pp, pbi = (gbi - 2) // 192, (gbi - 2) % 192
                    sy.wait_ge(H["pe"], peval(pp, pbi))
                    ins = sy.dma_start(out=wblk[k % 2][:, 0:D], in_=w3b[k])
                    wld[("w3", p, sub, k)] = (pn, inc(pn, ins, 16))
                for q4 in range(4):
                    u = (p * 2 + sub) * 4 + q4
                    if u >= 2:
                        up, uu = (u - 2) // 8, (u - 2) % 8
                        sy.wait_ge(H["dv"], dvv(up, 2, uu))
                    r0 = g0 + sub * 512 + q4 * 128
                    pn = "dYF%d" % (q4 % 2)
                    ins = sy.dma_start(out=yft[q4 % 2][:, :], in_=yf[r0:r0 + 128, :])
                    yfl[(p, sub, q4)] = (pn, inc(pn, ins, 16))

    @block.tensor
    def _(t):
        for p in range(NP):
            for si, (tag, wsrc, kk) in enumerate(STG):
                if si == 0:
                    t.wait_ge(H["dYT"], marks[("yT", p)])
                if si == 2:
                    t.wait_ge(H["dv"], dvv(p, 1, 0))
                if si == 3:
                    t.wait_ge(H["ac2"], ac2v(p, 1, 63))
                rhs = yTs if si < 2 else (Hb0 if si == 2 else Hb1)
                for c in range(HC):
                    bi = si * 32 + c
                    wm = wld[(p, si, c)]
                    t.wait_ge(H[wm[0]], wm[1])
                    for half in range(2):
                        mg = si * 64 + c * 2 + half
                        gmg = p * 256 + mg
                        if gmg >= 4:
                            pp, pmg = (gmg - 4) // 256, (gmg - 4) % 256
                            psi, pm = pmg // 64, pmg % 64
                            if psi == 1:
                                t.wait_ge(H["dv"], dvv(pp, 0, pm))
                            else:
                                t.wait_ge(H["ac2"], ac2v(pp, 0 if psi == 0 else psi - 1, pm))
                        elif p > 0:
                            t.wait_ge(H["dv"], dvv(p - 1, 2, 4 + (mg % 4) // 2))
                        b = pF[mg % 4]
                        for k in range(kk):
                            ins = t.matmul(b[:, :], wblk[c % 2][:, k * 128:k * 128 + 128],
                                           rhs[:, k * TP + half * 512:k * TP + half * 512 + 512],
                                           start=(k == 0), stop=(k == kk - 1))
                    assert inc("pe", ins) == peval(p, bi)
            t.wait_ge(H["ac2"], ac2v(p, 2, 63))
            for sub in range(2):
                for k in range(HC):
                    bi = 128 + sub * 32 + k
                    wm = wld[("w3", p, sub, k)]
                    t.wait_ge(H[wm[0]], wm[1])
                    for q4 in range(4):
                        if k == 0:
                            if sub == 1:
                                t.wait_ge(H["dv"], dvv(p, 2, q4))
                            elif p > 0:
                                t.wait_ge(H["dv"], dvv(p - 1, 2, 4 + q4))
                        for half in range(2):
                            ins = t.matmul(pF[2 * q4 + half][:, :],
                                           Hb0[:, k * TP + sub * 512 + q4 * 128:k * TP + sub * 512 + q4 * 128 + 128],
                                           wblk[k % 2][:, half * 512:half * 512 + 512],
                                           start=(k == 0), stop=(k == HC - 1))
                    assert inc("pe", ins) == peval(p, bi)

    @block.gpsimd
    def _(g):
        for p in range(NP):
            for sub in range(2):
                for q4 in range(4):
                    g.wait_ge(H["dv"], dvv(p, 4, sub * 4 + q4))
                    sq = (p % 2) * 1024 + sub * 512 + q4 * 128
                    ins = g.dma_start(out=out[p // 2, sq:sq + 128, :], in_=r2[:, q4 * D:q4 * D + D])
                    inc("dO", ins, 16)
                outst[(p, sub)] = n["dO"]
        g.wait_ge(H["dO"], n["dO"])

    @block.scalar
    def _(sc):
        for p in range(NP):
            for si in (0, 2, 3):
                kind = 0 if si == 0 else si - 1
                dst = {0: Hb0, 2: Hb1, 3: Hb0}[si]
                for c in range(HC):
                    sc.wait_ge(H["pe"], peval(p, si * 32 + c))
                    for half in range(2):
                        mg = c * 2 + half
                        if si == 0:
                            sap = s12_sb[:, c:c + 1]
                            fn = AF.Silu
                        elif si == 2:
                            sap = sh12_sb[:, c:c + 1]
                            fn = AF.Gelu
                        else:
                            sap = sh12_sb[:, HC + c:HC + c + 1]
                            fn = AF.Gelu
                        ins = sc.activation(dst[:, c * TP + half * 512:c * TP + half * 512 + 512],
                                            pF[(si * 64 + mg) % 4][:, :], fn, scale=sap)
                        assert inc("ac2", ins) == ac2v(p, kind, mg)
            for sub in range(2):
                sc.wait_ge(H["dv"], dvv(p, 3, sub))
                aflush(sc, sc.activation(st2[:, 16:20], st2[:, 20:24], AF.Ln, bias=cst[:, 2:3]))
                ins = sc.activation(st2[:, 24:28], st2[:, 16:20], AF.Exp, scale=cst[:, 3:4])
                assert inc("ac2", ins) == ac2v(p, 3, sub)

    @block.vector
    def _(v):
        for p in range(NP):
            for c in range(HC):
                v.wait_ge(H["pe"], peval(p, 32 + c))
                for half in range(2):
                    mg = 64 + c * 2 + half
                    ins = v.tensor_single_scalar(
                        Hb1[:, c * TP + half * 512:c * TP + half * 512 + 512],
                        pF[mg % 4][:, :], s12_sb[:, HC + c:HC + c + 1], ALU.mult)
                    assert inc("dv", ins) == dvv(p, 0, c * 2 + half)
                v.wait_ge(H["ac2"], ac2v(p, 0, c * 2 + 1))
                v.wait_ge(H["dv"], dvv(p, 0, c * 2 + 1))
                ins = v.tensor_tensor(Hb0[:, c * TP:(c + 1) * TP], Hb0[:, c * TP:(c + 1) * TP],
                                      Hb1[:, c * TP:(c + 1) * TP], ALU.mult)
                if c == HC - 1:
                    assert inc("dv", ins) == dvv(p, 1, 0)
            for sub in range(2):
                v.wait_ge(H["pe"], peval(p, 128 + sub * 32 + 31))
                if p > 0 or sub > 0:
                    psub = (p * 2 + sub - 1)
                    v.wait_ge(H["dO"], outst[(psub // 2, psub % 2)])
                for q4 in range(4):
                    for half in range(2):
                        ins = v.tensor_tensor(r2[:, q4 * D + half * 512:q4 * D + half * 512 + 512],
                                              pF[2 * q4 + half][:, :],
                                              s3b_sb[:, half * 512:half * 512 + 512], ALU.mult)
                    vflush(v, ins)
                    ym = yfl[(p, sub, q4)]
                    v.wait_ge(H[ym[0]], ym[1])
                    v.scalar_tensor_tensor(r2[:, q4 * D:q4 * D + 512], r2[:, q4 * D:q4 * D + 512],
                                           1.0, yft[q4 % 2][:, 0:512], ALU.mult, ALU.add,
                                           accum_out=st2[:, q4:q4 + 1])
                    ins = v.scalar_tensor_tensor(r2[:, q4 * D + 512:q4 * D + 1024],
                                                 r2[:, q4 * D + 512:q4 * D + 1024],
                                                 1.0, yft[q4 % 2][:, 512:1024], ALU.mult, ALU.add,
                                                 accum_out=st2[:, 4 + q4:5 + q4])
                    assert inc("dv", ins) == dvv(p, 2, sub * 4 + q4)
                    v.wait_ge(H["dv"], n["dv"])
                    ins = v.scalar_tensor_tensor(sqd2[:, :], r2[:, q4 * D:q4 * D + D], 1.0,
                                                 r2[:, q4 * D:q4 * D + D], ALU.mult, ALU.mult,
                                                 accum_out=st2[:, 8 + q4:9 + q4])
                vflush(v, ins)
                vflush(v, v.tensor_tensor(st2[:, 12:16], st2[:, 0:4], st2[:, 4:8], ALU.add))
                vflush(v, v.tensor_single_scalar(st2[:, 12:16], st2[:, 12:16], 1.0 / D, ALU.mult))
                vflush(v, v.tensor_tensor(st2[:, 16:20], st2[:, 12:16], st2[:, 12:16], ALU.mult))
                vflush(v, v.tensor_single_scalar(st2[:, 20:24], st2[:, 8:12], 1.0 / D, ALU.mult))
                ins = v.tensor_tensor(st2[:, 20:24], st2[:, 20:24], st2[:, 16:20], ALU.subtract)
                assert inc("dv", ins) == dvv(p, 3, sub)
                v.wait_ge(H["ac2"], ac2v(p, 3, sub))
                vflush(v, v.tensor_tensor(st2[:, 28:32], st2[:, 12:16], st2[:, 24:28], ALU.mult))
                for q4 in range(4):
                    ins = v.tensor_scalar(r2[:, q4 * D:q4 * D + D], r2[:, q4 * D:q4 * D + D],
                                          st2[:, 24 + q4:25 + q4], st2[:, 28 + q4:29 + q4],
                                          ALU.mult, ALU.subtract)
                    assert inc("dv", ins) == dvv(p, 4, sub * 4 + q4)

    ffn.close()
    ffps.close()
    ctx.close()
    return nc


_NC_CACHE = {}


def kernel(L0_attn_x, L0_attn_wq_w, L0_attn_wq_b, L0_attn_wk_w, L0_attn_wk_b,
           L0_attn_wv_w, L0_attn_wv_b, L0_attn_wo_w, L0_attn_wo_b,
           L0_ffn_w1_w, L0_ffn_w1_s, L0_ffn_w2_w, L0_ffn_w2_s,
           L0_ffn_h1_w, L0_ffn_h1_s, L0_ffn_h2_w, L0_ffn_h2_s,
           L0_ffn_w3_w, L0_ffn_w3_s):
    f = np.float32
    x = np.asarray(L0_attn_x, f)

    def kslab(w):
        return np.ascontiguousarray(np.asarray(w, f).T.reshape(DC, 128, D).transpose(1, 0, 2))

    def cblk(w, kk):
        w = np.asarray(w, f)
        return np.ascontiguousarray(
            w.reshape(HC, 128, kk, 128).transpose(0, 3, 2, 1).reshape(HC, 128, kk * 128))

    def pcol(b):
        return np.ascontiguousarray(np.asarray(b, f).reshape(DC, 128).T)

    def brow(b):
        return np.ascontiguousarray(np.broadcast_to(np.asarray(b, f), (128, D)))

    def scol(sv):
        return np.ascontiguousarray(np.asarray(sv, f).reshape(HC, 128).T)

    shared = {
        "wq": kslab(L0_attn_wq_w), "wk": kslab(L0_attn_wk_w),
        "wv": kslab(L0_attn_wv_w), "wo": kslab(L0_attn_wo_w),
        "bqp": pcol(L0_attn_wq_b), "bkp": pcol(L0_attn_wk_b),
        "bvb": brow(L0_attn_wv_b), "bob": brow(L0_attn_wo_b),
        "w1": cblk(L0_ffn_w1_w, DC), "w2": cblk(L0_ffn_w2_w, DC),
        "h1": cblk(L0_ffn_h1_w, HC), "h2": cblk(L0_ffn_h2_w, HC),
        "w3": np.ascontiguousarray(np.asarray(L0_ffn_w3_w, f).T.reshape(HC, 128, D)),
        "s12": np.ascontiguousarray(np.concatenate([scol(L0_ffn_w1_s), scol(L0_ffn_w2_s)], axis=1)),
        "sh12": np.ascontiguousarray(np.concatenate([scol(L0_ffn_h1_s), scol(L0_ffn_h2_s)], axis=1)),
        "s3b": np.ascontiguousarray(np.broadcast_to(np.asarray(L0_ffn_w3_s, f), (128, D))),
        "idn": np.eye(128, dtype=f),
    }
    in_maps = []
    for c in range(NCORES):
        xs = x[SEQ * c:SEQ * (c + 1)]
        m = dict(shared)
        m["xr"] = np.ascontiguousarray(xs)
        m["xT"] = np.ascontiguousarray(xs.transpose(0, 2, 1).reshape(SEQ, DC, 128, S))
        in_maps.append(m)

    if "nc" not in _NC_CACHE:
        _NC_CACHE["nc"] = build()
    res = run_bass_kernel_spmd(_NC_CACHE["nc"], in_maps, core_ids=list(range(NCORES)))
    return np.concatenate([r["out"] for r in res.results], axis=0)


# version 09
# speedup vs baseline: 1.1940x; 1.1940x over previous
import numpy as np
import concourse.bass as bass
import concourse.mybir as mybir
from concourse.bass_utils import run_bass_kernel_spmd
from contextlib import ExitStack

F32 = mybir.dt.float32
BF16 = mybir.dt.bfloat16
AF = mybir.ActivationFunctionType
ALU = mybir.AluOpType

SEQ, S, D, NH, DH = 2, 2048, 1024, 16, 4096
DC, HC, KC, QCH = D // 128, DH // 128, S // 128, S // 512
NP, TP = 4, 1024
EXPS = -20.0
EPS = 1e-5
NCORES = 8


def build():
    nc = bass.Bass("TRN2", target_bir_lowering=False)
    dt = nc.dram_tensor
    xT = dt("xT", [SEQ, DC, 128, S], F32, kind="ExternalInput").ap()
    xr = dt("xr", [SEQ, S, D], F32, kind="ExternalInput").ap()
    wq = dt("wq", [128, DC, D], F32, kind="ExternalInput").ap()
    wk = dt("wk", [128, DC, D], F32, kind="ExternalInput").ap()
    wv = dt("wv", [128, DC, D], F32, kind="ExternalInput").ap()
    wo = dt("wo", [128, DC, D], F32, kind="ExternalInput").ap()
    bqp = dt("bqp", [128, DC], F32, kind="ExternalInput").ap()
    bkp = dt("bkp", [128, DC], F32, kind="ExternalInput").ap()
    bvb = dt("bvb", [128, D], F32, kind="ExternalInput").ap()
    bob = dt("bob", [128, D], F32, kind="ExternalInput").ap()
    w1 = dt("w1", [HC, 128, DC * 128], F32, kind="ExternalInput").ap()
    w2 = dt("w2", [HC, 128, DC * 128], F32, kind="ExternalInput").ap()
    h1 = dt("h1", [HC, 128, HC * 128], F32, kind="ExternalInput").ap()
    h2 = dt("h2", [HC, 128, HC * 128], F32, kind="ExternalInput").ap()
    w3 = dt("w3", [HC, 128, D], F32, kind="ExternalInput").ap()
    s12 = dt("s12", [128, 2 * HC], F32, kind="ExternalInput").ap()
    sh12 = dt("sh12", [128, 2 * HC], F32, kind="ExternalInput").ap()
    s3b = dt("s3b", [128, D], F32, kind="ExternalInput").ap()
    idn = dt("idn", [128, 128], F32, kind="ExternalInput").ap()
    out = dt("out", [SEQ, S, D], F32, kind="ExternalOutput").ap()

    w1b = dt("w1b", [HC, 128, DC * 128], BF16, kind="Internal").ap()
    w2b = dt("w2b", [HC, 128, DC * 128], BF16, kind="Internal").ap()
    h1b = dt("h1b", [HC, 128, HC * 128], BF16, kind="Internal").ap()
    h2b = dt("h2b", [HC, 128, HC * 128], BF16, kind="Internal").ap()
    w3b = dt("w3b", [HC, 128, D], BF16, kind="Internal").ap()
    wqkvb = dt("wqkvb", [3, 128, DC * D], BF16, kind="Internal").ap()
    xT1b = dt("xT1b", [DC * 128, S], BF16, kind="Internal").ap()
    yf = dt("yf", [SEQ * S, D], F32, kind="Internal").ap()
    yb = dt("yb", [SEQ * S, D], BF16, kind="Internal").ap()

    ctx = ExitStack()
    sb = lambda n_, s_, d_: ctx.enter_context(nc.sbuf_tensor(n_, s_, d_))
    cst = sb("cst", [128, 8], F32)
    ones = sb("ones", [1, 512], BF16)
    bq_sb = sb("bq_sb", [128, DC], F32)
    bk_sb = sb("bk_sb", [128, DC], F32)
    s12_sb = sb("s12_sb", [128, 2 * HC], F32)
    sh12_sb = sb("sh12_sb", [128, 2 * HC], F32)
    s3b_sb = sb("s3b_sb", [128, D], F32)

    names = ["dSm", "dXT", "dWA", "dRL", "dC", "dXC", "dO", "pe", "ex", "dv", "ac2",
             "vself", "aself",
             "dY00", "dY01", "dY10", "dY11", "dXR0", "dXR1",
             "dWB0", "dWB1", "dYT", "dYF0", "dYF1"]
    H = {x: ctx.enter_context(nc.semaphore(x)) for x in names}
    n = {x: 0 for x in names}

    def inc(name, ins, k=1):
        ins.then_inc(H[name], k)
        n[name] += k
        return n[name]

    def vflush(v, ins):
        v.wait_ge(H["vself"], inc("vself", ins))

    def aflush(sc, ins):
        sc.wait_ge(H["aself"], inc("aself", ins))

    block = ctx.enter_context(nc.Block())
    marks = {}
    ystore = [None] * (SEQ * 16)

    att = ExitStack()
    sA = lambda n_, s_, d_: att.enter_context(nc.sbuf_tensor(n_, s_, d_))
    woT = sA("woT", [128, DC * D], BF16)
    identb = sA("identb", [128, 128], F32)
    bvb_sb = sA("bvb_sb", [128, D], F32)
    bob_sb = sA("bob_sb", [128, D], F32)
    QT = sA("QT", [128, DC * S], BF16)
    KT = sA("KT", [128, DC * S], BF16)
    Vg = sA("Vg", [128, KC * NH * 65], BF16)
    Vg4 = Vg.rearrange("p (j h e) -> p j h e", j=KC, h=NH)

    attps = ExitStack()
    pS = [attps.enter_context(nc.psum_tensor(f"pS{i}", [128, 1024], F32)) for i in range(2)]
    pV = [attps.enter_context(nc.psum_tensor(f"pV{i}", [128, 512], F32)) for i in range(2)]
    pAV = [attps.enter_context(nc.psum_tensor(f"pAV{i}", [128, 260], F32)) for i in range(2)]

    def cast_chunks(g, dst, src, tag):
        cc, pp, ff = src.shape
        fi = src.rearrange("c p f -> (c p) f")
        fo = dst.rearrange("c p f -> (c p) f")
        step = max(1, (8 << 20) // (ff * 4))
        r = 0
        while r < cc * pp:
            e = min(cc * pp, r + step)
            inc("dC", g.dma_start(out=fo[r:e, :], in_=fi[r:e, :]), 16)
            r = e
        marks[tag] = n["dC"]

    @block.gpsimd
    def _(g):
        for d_, s_ in [(bq_sb, bqp), (bk_sb, bkp), (bvb_sb, bvb), (bob_sb, bob),
                       (s12_sb, s12), (sh12_sb, sh12), (s3b_sb, s3b), (identb, idn)]:
            inc("dSm", g.dma_start(out=d_[:, :], in_=s_), 16)
        marks["small"] = n["dSm"]

    @block.vector
    def _(v):
        v.memset(cst[:, 0:1], 0.125)
        v.memset(cst[:, 1:2], EXPS)
        v.memset(cst[:, 2:3], EPS)
        v.memset(cst[:, 3:4], -0.5)
        v.memset(ones[:, :], 1.0)
        inc("dv", v.memset(Vg4[:, :, :, 64:65], 1.0))

    B_dv0, B_ex0 = {}, {}

    def headev(s, qc, h):
        return B_dv0[s] + qc * 29 + h + 1

    def attTev(s, qc, qt):
        return B_dv0[s] + qc * 29 + 16 + 2 * qt + 1

    def res1ev(s, qc, qt):
        return B_dv0[s] + qc * 29 + 16 + 2 * qt + 2

    def varev(s, qc):
        return B_dv0[s] + qc * 29 + 25

    def ynorm(s, qc, qt):
        return B_dv0[s] + qc * 29 + 25 + qt + 1

    def exv(s, qc, hp, j):
        return B_ex0[s] + (qc * 8 + hp) * KC + j + 1

    for s in range(SEQ):
        pha = ExitStack()
        XTs = pha.enter_context(nc.sbuf_tensor(f"XTs{s}", [128, DC * S], BF16))
        wbuf = [pha.enter_context(nc.sbuf_tensor(f"wbuf{s}_{i}", [128, DC * D], BF16))
                for i in range(2)]

        if s == 1:
            marks["wq1"] = 48
            marks["wk1"] = 48
            marks["wv1"] = 64
            marks["xT1"] = 48

        @block.gpsimd
        def _(g, s=s, XTs=XTs, wbuf=wbuf):
            if s == 0:
                for k in range(DC):
                    inc("dXT", g.dma_start(out=XTs[:, k * S:(k + 1) * S], in_=xT[0, k]), 16)
                marks["xT0"] = n["dXT"]
                inc("dWA", g.dma_start(out=wbuf[0][:, :], in_=wq.rearrange("p k o -> p (k o)")), 16)
                inc("dWA", g.dma_start(out=wbuf[1][:, :], in_=wk.rearrange("p k o -> p (k o)")), 16)
                inc("dWA", g.dma_start(out=woT[:, :], in_=wo.rearrange("p k o -> p (k o)")), 16)
                marks["wqkwo"] = n["dWA"]
                g.wait_ge(H["dWA"], marks["wqkwo"])
                inc("dWA", g.dma_start(out=wqkvb[0], in_=wbuf[0][:, :]), 16)
                inc("dWA", g.dma_start(out=wqkvb[1], in_=wbuf[1][:, :]), 16)
                marks["wqkv_st12"] = n["dWA"]
                cast_chunks(g, w1b, w1, "w1b")
                cast_chunks(g, w2b, w2, "w2b")
                inc("dXC", g.dma_start(out=xT1b,
                                       in_=xT[1].rearrange("k p t -> (k p) t")), 16)
                marks["xT1b"] = n["dXC"]
            else:
                pass

        dvA0 = n["dv"]
        if s == 0:
            marks["wv0"] = n["dWA"] + 16

        @block.tensor
        def _(t, s=s, XTs=XTs, wbuf=wbuf):
            if s == 0:
                t.wait_ge(H["dXT"], marks["xT0"])
                t.wait_ge(H["dSm"], marks["small"])
                t.wait_ge(H["dWA"], marks["wqkwo"])
            else:
                t.wait_ge(H["dRL"], 48)
            for w_i in range(2):
                for oc in range(DC):
                    for t4 in range(4):
                        i = w_i * 32 + oc * 4 + t4
                        if i >= 2:
                            t.wait_ge(H["dv"], dvA0 + (i - 2) + 1)
                        b = pS[i % 2][:, 0:512]
                        for k in range(DC):
                            ins = t.matmul(b,
                                           wbuf[w_i][:, k * D + oc * 128:k * D + oc * 128 + 128],
                                           XTs[:, k * S + t4 * 512:k * S + t4 * 512 + 512],
                                           start=(k == 0), stop=(k == DC - 1))
                        marks[("qk", s, i)] = inc("pe", ins)
            t.wait_ge(H["dWA"] if s == 0 else H["dRL"],
                      marks["wv0"] if s == 0 else marks["wv1"])
            for jj in range(KC):
                if jj >= 1:
                    t.wait_ge(H["dv"], dvA0 + 64 + (jj - 1) + 1)
                b0, b1 = pV[0], pV[1]
                for k in range(DC):
                    t.matmul(b0[:, :], XTs[:, k * S + jj * 128:k * S + jj * 128 + 128],
                             wbuf[0][:, k * D:k * D + 512], start=(k == 0), stop=(k == DC - 1))
                    ins = t.matmul(b1[:, :], XTs[:, k * S + jj * 128:k * S + jj * 128 + 128],
                                   wbuf[0][:, k * D + 512:k * D + 1024],
                                   start=(k == 0), stop=(k == DC - 1))
                marks[("v", s, jj)] = inc("pe", ins)
            marks[f"Vmm{s}"] = n["pe"]

        if s == 0:
            @block.gpsimd
            def _(g, wbuf=wbuf):
                g.wait_ge(H["pe"], marks[("qk", 0, 31)])
                g.wait_ge(H["dWA"], marks["wqkv_st12"])
                assert inc("dWA", g.dma_start(out=wbuf[0][:, :],
                                              in_=wv.rearrange("p k o -> p (k o)")), 16) == marks["wv0"]
                g.wait_ge(H["dWA"], marks["wv0"])
                inc("dWA", g.dma_start(out=wqkvb[2], in_=wbuf[0][:, :]), 16)
                marks["wqkv_st"] = n["dWA"]
                cast_chunks(g, h1b, h1, "h1b")
                cast_chunks(g, h2b, h2, "h2b")
                cast_chunks(g, w3b, w3, "w3b")
        else:
            @block.sync
            def _(sy, wbuf=wbuf, XTs=XTs):
                sy.wait_ge(H["dWA"], marks["wqkv_st"])
                sy.wait_ge(H["dXC"], marks["xT1b"])
                sy.wait_ge(H["ex"], marks["B0_ex_end"])
                sy.wait_ge(H["pe"], marks[("tr", 0, QCH - 1, 3)])
                sy.wait_ge(H["dv"], marks["B0_hd_end"])
                inc("dRL", sy.dma_start(
                    out=XTs[:, :].rearrange("p (k t) -> p k t", k=DC),
                    in_=xT1b.rearrange("(k p) t -> p k t", p=128)), 16)
                sy.wait_ge(H["dv"], marks["B0_dv_end"])
                inc("dRL", sy.dma_start(out=wbuf[0][:, :], in_=wqkvb[0]), 16)
                inc("dRL", sy.dma_start(out=wbuf[1][:, :], in_=wqkvb[1]), 16)
                sy.wait_ge(H["pe"], marks[("qk", 1, 31)])
                assert inc("dRL", sy.dma_start(out=wbuf[0][:, :], in_=wqkvb[2]), 16) == marks["wv1"]

        @block.vector
        def _(v, s=s):
            for w_i in range(2):
                dst = QT if w_i == 0 else KT
                bia = bq_sb if w_i == 0 else bk_sb
                for oc in range(DC):
                    for t4 in range(4):
                        i = w_i * 32 + oc * 4 + t4
                        v.wait_ge(H["pe"], marks[("qk", s, i)])
                        ins = v.tensor_single_scalar(
                            dst[:, oc * S + t4 * 512:oc * S + t4 * 512 + 512],
                            pS[i % 2][:, 0:512], bia[:, oc:oc + 1], ALU.add)
                        assert inc("dv", ins) == dvA0 + i + 1
            bvb3 = bvb_sb.rearrange("p (h e) -> p h e", e=64)
            for jj in range(KC):
                v.wait_ge(H["pe"], marks[("v", s, jj)])
                for half in range(2):
                    bb = pV[half]
                    base = jj * NH * 65 + half * 8 * 65
                    dst = Vg[:, base:base + 8 * 65].rearrange("p (h e) -> p h e", e=65)[:, :, 0:64]
                    ins = v.tensor_tensor(
                        dst,
                        bb[:, :].rearrange("p (h e) -> p h e", h=8),
                        bvb3[:, half * 8:half * 8 + 8, :],
                        ALU.add)
                assert inc("dv", ins) == dvA0 + 64 + jj + 1

        pha.close()

        phb = ExitStack()
        sB = lambda n_, s_, d_: phb.enter_context(nc.sbuf_tensor(n_, s_, d_))
        E = sB(f"E{s}", [128, 2 * 1024], BF16)
        att_o = [sB(f"att_o{s}_{i}", [128, 4 * D], F32) for i in range(2)]
        attT = sB(f"attT{s}", [128, 4 * D], BF16)
        r1 = sB(f"r1{s}", [128, 4 * D], F32)
        xrb = [sB(f"xrb{s}_{i}", [128, D], F32) for i in range(2)]
        yt = [sB(f"yt{s}_{i}", [128, D], F32) for i in range(2)]
        sqd = sB(f"sqd{s}", [128, D], F32)
        st1 = sB(f"st1{s}", [128, 32], F32)
        rz = sB(f"rz{s}", [128, 4], F32)

        B_dv0[s] = n["dv"]
        B_ex0[s] = n["ex"]
        aev = n["dv"]
        for qc_ in range(QCH):
            for qt_ in range(4):
                i_ = qc_ * 4 + qt_
                pn = "dXR%d" % (i_ % 2)
                marks[("xr", s, qc_, qt_)] = (pn, n[pn] + (i_ // 2 + 1) * 16)
        for gi in range(16):
            pn = "dY%d%d" % (s, gi % 2)
            ystore[s * 16 + gi] = (pn, n[pn] + (gi // 2 + 1) * 32)

        P = {}
        dvp = [n["dv"]]

        def assign(key):
            dvp[0] += 1
            P[key] = dvp[0]

        for qc_ in range(QCH):
            for hp_ in range(8):
                if qc_ >= 1:
                    if hp_ in (1, 3, 5, 7):
                        assign(("aT", qc_ - 1, (hp_ - 1) // 2))
                    elif hp_ in (2, 4, 6):
                        assign(("r1", qc_ - 1, hp_ // 2 - 1))
                assign(("hd", qc_, 2 * hp_))
                assign(("hd", qc_, 2 * hp_ + 1))
            if qc_ >= 1:
                assign(("r1", qc_ - 1, 3))
                assign(("var", qc_ - 1))
                for qt_ in range(4):
                    assign(("yn", qc_ - 1, qt_))
        ql = QCH - 1
        for qt_ in range(4):
            assign(("aT", ql, qt_))
            assign(("r1", ql, qt_))
        assign(("var", ql))
        for qt_ in range(4):
            assign(("yn", ql, qt_))

        def headev(s_, qc_, h_):
            return P[("hd", qc_, h_)]

        def attTev(s_, qc_, qt_):
            return P[("aT", qc_, qt_)]

        def res1ev(s_, qc_, qt_):
            return P[("r1", qc_, qt_)]

        def varev(s_, qc_):
            return P[("var", qc_)]

        def ynorm(s_, qc_, qt_):
            return P[("yn", qc_, qt_)]

        @block.tensor
        def _(t, s=s, E=E, att_o=att_o, attT=attT):
            t.wait_ge(H["dv"], aev)

            def spair(qc, hp, j):
                gg = hp * KC + j
                if gg >= 2:
                    t.wait_ge(H["ex"], exv(s, qc, (gg - 2) // KC, (gg - 2) % KC))
                elif qc > 0:
                    t.wait_ge(H["ex"], exv(s, qc - 1, 7, KC - 2 + gg))
                for hh in range(2):
                    h = 2 * hp + hh
                    dcx, pb = (h // 2) * S, 64 * hh
                    ins = t.matmul(pS[j % 2][:, hh * 512:hh * 512 + 512],
                                   KT[pb:pb + 64, dcx + j * 128:dcx + j * 128 + 128],
                                   QT[pb:pb + 64, dcx + qc * 512:dcx + qc * 512 + 512],
                                   start=True, stop=True)
                marks[("s", s, qc, hp, j)] = inc("pe", ins)

            def avpair(qc, hp, j):
                t.wait_ge(H["ex"], exv(s, qc, hp, j))
                for hh in range(2):
                    if j == 0:
                        ph = 2 * (hp - 1) + hh
                        if hp > 0:
                            t.wait_ge(H["dv"], headev(s, qc, ph))
                        elif qc > 0:
                            t.wait_ge(H["dv"], headev(s, qc - 1, NH - 2 + hh))
                    h = 2 * hp + hh
                    voff = j * NH * 65 + 65 * h
                    for qt in range(4):
                        ins = t.matmul(pAV[hh][:, qt * 65:qt * 65 + 65],
                                       E[:, (j % 2) * 1024 + hh * 512 + qt * 128:
                                          (j % 2) * 1024 + hh * 512 + qt * 128 + 128],
                                       Vg[:, voff:voff + 65],
                                       start=(j == 0 and qt == 0), stop=(j == KC - 1 and qt == 3))
                marks[("av", s, qc, hp, j)] = inc("pe", ins)

            def tr_step(qc2, qt):
                if qt == 0:
                    t.wait_ge(H["dv"], headev(s, qc2, NH - 1))
                    if qc2 > 0:
                        t.wait_ge(H["dv"], res1ev(s, qc2 - 1, 3))
                else:
                    t.wait_ge(H["dv"], res1ev(s, qc2, qt - 1))
                for k in range(DC):
                    ins = t.transpose(pV[k // 4][:, (k % 4) * 128:(k % 4) * 128 + 128],
                                      att_o[qc2 % 2][:, qt * D + k * 128:qt * D + k * 128 + 128],
                                      identb[:, :])
                marks[("tr", s, qc2, qt)] = inc("pe", ins)

            def wo_step(qc2, qt):
                t.wait_ge(H["dv"], attTev(s, qc2, qt))
                for k in range(DC):
                    t.matmul(pV[0][:, :], attT[:, qt * D + k * 128:qt * D + k * 128 + 128],
                             woT[:, k * D:k * D + 512], start=(k == 0), stop=(k == DC - 1))
                    ins = t.matmul(pV[1][:, :], attT[:, qt * D + k * 128:qt * D + k * 128 + 128],
                                   woT[:, k * D + 512:k * D + 1024],
                                   start=(k == 0), stop=(k == DC - 1))
                marks[("wom", s, qc2, qt)] = inc("pe", ins)

            for qc in range(QCH):
                for hp in range(8):
                    for j in range(KC):
                        spair(qc, hp, j)
                        if j > 0:
                            avpair(qc, hp, j - 1)
                    avpair(qc, hp, KC - 1)
                    if qc >= 1:
                        if hp % 2 == 0 and hp < 8:
                            tr_step(qc - 1, hp // 2)
                        else:
                            wo_step(qc - 1, (hp - 1) // 2)
            for qt in range(4):
                tr_step(QCH - 1, qt)
                wo_step(QCH - 1, qt)

        @block.scalar
        def _(sc, s=s, E=E):
            def ln1(qc2):
                sc.wait_ge(H["dv"], varev(s, qc2))
                aflush(sc, sc.activation(st1[:, 16:20], st1[:, 20:24], AF.Ln, bias=cst[:, 2:3]))
                ins = sc.activation(st1[:, 24:28], st1[:, 16:20], AF.Exp, scale=cst[:, 3:4])
                marks[("rstd", s, qc2)] = inc("ac2", ins)

            for qc in range(QCH):
                for hp in range(8):
                    for j in range(KC):
                        sc.wait_ge(H["pe"], marks[("s", s, qc, hp, j)])
                        gg = hp * KC + j
                        if gg >= 2:
                            sc.wait_ge(H["pe"], marks[("av", s, qc, (gg - 2) // KC, (gg - 2) % KC)])
                        elif qc > 0:
                            sc.wait_ge(H["pe"], marks[("av", s, qc - 1, 7, KC - 2 + gg)])
                        ins = sc.activation(E[:, (j % 2) * 1024:(j % 2) * 1024 + 1024],
                                            pS[j % 2][:, :],
                                            AF.Exp, bias=cst[:, 1:2], scale=cst[:, 0:1])
                        assert inc("ex", ins) == exv(s, qc, hp, j)
                if qc >= 1:
                    ln1(qc - 1)
            ln1(QCH - 1)

        @block.vector
        def _(v, s=s, E=E, att_o=att_o, attT=attT, xrb=xrb, yt=yt, sqd=sqd, st1=st1, rz=rz, r1=r1):
            if s == 0:
                v.wait_ge(H["dWA"], marks["wqkv_st"])

            def head_evac(qc2, h):
                v.wait_ge(H["pe"], marks[("av", s, qc2, h // 2, KC - 1)])
                if h > 0 or qc2 > 0 or s > 0:
                    v.wait_ge(H["dv"], n["dv"])
                for qt in range(4):
                    ins = v.reciprocal(rz[:, qt:qt + 1],
                                       pAV[h % 2][:, qt * 65 + 64:qt * 65 + 65])
                vflush(v, ins)
                for qt in range(4):
                    ins = v.tensor_single_scalar(
                        att_o[qc2 % 2][:, qt * D + 64 * h:qt * D + 64 * h + 64],
                        pAV[h % 2][:, qt * 65:qt * 65 + 64], rz[:, qt:qt + 1], ALU.mult)
                assert inc("dv", ins) == headev(s, qc2, h)

            def attT_evac(qc2, qt):
                v.wait_ge(H["pe"], marks[("tr", s, qc2, qt)])
                v.tensor_copy(attT[:, qt * D:qt * D + 512], pV[0][:, :])
                ins = v.tensor_copy(attT[:, qt * D + 512:qt * D + 1024], pV[1][:, :])
                assert inc("dv", ins) == attTev(s, qc2, qt)

            def res1_step(qc2, qt):
                xrm = marks[("xr", s, qc2, qt)]
                v.wait_ge(H[xrm[0]], xrm[1])
                vflush(v, v.tensor_tensor(xrb[qt % 2][:, :], xrb[qt % 2][:, :], bob_sb[:, :], ALU.add))
                v.wait_ge(H["pe"], marks[("wom", s, qc2, qt)])
                v.scalar_tensor_tensor(r1[:, qt * D:qt * D + 512], pV[0][:, :], 1.0,
                                       xrb[qt % 2][:, 0:512], ALU.mult, ALU.add,
                                       accum_out=st1[:, qt:qt + 1])
                ins = v.scalar_tensor_tensor(r1[:, qt * D + 512:qt * D + 1024], pV[1][:, :], 1.0,
                                             xrb[qt % 2][:, 512:1024], ALU.mult, ALU.add,
                                             accum_out=st1[:, 4 + qt:5 + qt])
                assert inc("dv", ins) == res1ev(s, qc2, qt)
                v.wait_ge(H["dv"], n["dv"])
                ins = v.scalar_tensor_tensor(sqd[:, :], r1[:, qt * D:qt * D + D], 1.0,
                                             r1[:, qt * D:qt * D + D], ALU.mult, ALU.mult,
                                             accum_out=st1[:, 8 + qt:9 + qt])
                vflush(v, ins)

            def ln1_post(qc2):
                vflush(v, v.tensor_tensor(st1[:, 12:16], st1[:, 0:4], st1[:, 4:8], ALU.add))
                vflush(v, v.tensor_single_scalar(st1[:, 12:16], st1[:, 12:16], 1.0 / D, ALU.mult))
                vflush(v, v.tensor_tensor(st1[:, 16:20], st1[:, 12:16], st1[:, 12:16], ALU.mult))
                vflush(v, v.tensor_single_scalar(st1[:, 20:24], st1[:, 8:12], 1.0 / D, ALU.mult))
                ins = v.tensor_tensor(st1[:, 20:24], st1[:, 20:24], st1[:, 16:20], ALU.subtract)
                assert inc("dv", ins) == varev(s, qc2)
                v.wait_ge(H["ac2"], marks[("rstd", s, qc2)])
                vflush(v, v.tensor_tensor(st1[:, 28:32], st1[:, 12:16], st1[:, 24:28], ALU.mult))
                for qt in range(4):
                    g = s * 16 + qc2 * 4 + qt
                    if g >= 2:
                        ym = ystore[g - 2]
                        v.wait_ge(H[ym[0]], ym[1])
                    ins = v.tensor_scalar(yt[qt % 2][:, :], r1[:, qt * D:qt * D + D],
                                          st1[:, 24 + qt:25 + qt], st1[:, 28 + qt:29 + qt],
                                          ALU.mult, ALU.subtract)
                    assert inc("dv", ins) == ynorm(s, qc2, qt)

            for qc in range(QCH):
                for hp in range(8):
                    if qc >= 1:
                        if hp in (1, 3, 5, 7):
                            attT_evac(qc - 1, (hp - 1) // 2)
                        elif hp in (2, 4, 6):
                            res1_step(qc - 1, hp // 2 - 1)
                    head_evac(qc, 2 * hp)
                    head_evac(qc, 2 * hp + 1)
                if qc >= 1:
                    res1_step(qc - 1, 3)
                    ln1_post(qc - 1)
            ql2 = QCH - 1
            for qt in range(4):
                attT_evac(ql2, qt)
                res1_step(ql2, qt)
            ln1_post(ql2)

        @block.sync
        def _(sy, s=s, xrb=xrb):
            if s == 0:
                sy.wait_ge(H["dWA"], marks["wqkv_st"])
            for qc in range(QCH):
                for qt in range(4):
                    i = qc * 4 + qt
                    if i >= 2:
                        sy.wait_ge(H["dv"], res1ev(s, (i - 2) // 4, (i - 2) % 4))
                    else:
                        sy.wait_ge(H["dv"], B_dv0[s])
                    r0 = qc * 512 + qt * 128
                    ins = sy.dma_start(out=xrb[qt % 2][:, :], in_=xr[s, r0:r0 + 128, :])
                    xrm = marks[("xr", s, qc, qt)]
                    assert inc(xrm[0], ins, 16) == xrm[1]

        @block.gpsimd
        def _(g, s=s, yt=yt):
            for qc in range(QCH):
                for qt in range(4):
                    gidx = s * 16 + qc * 4 + qt
                    r0 = s * S + qc * 512 + qt * 128
                    g.wait_ge(H["dv"], ynorm(s, qc, qt))
                    pn = ystore[gidx][0]
                    inc(pn, g.dma_start(out=yf[r0:r0 + 128, :], in_=yt[qt % 2][:, :]), 16)
                    inc(pn, g.dma_start(out=yb[r0:r0 + 128, :], in_=yt[qt % 2][:, :]), 16)
                    assert n[pn] == ystore[gidx][1]

        assert n["dv"] == dvp[0], (n["dv"], dvp[0])

        phb.close()
        if s == 0:
            marks["B0_dv_end"] = n["dv"]
            marks["B0_ex_end"] = n["ex"]
            marks["B0_hd_end"] = P[("hd", QCH - 1, NH - 1)]

    att_dv_done = n["dv"]
    att.close()
    attps.close()

    ffn = ExitStack()
    sF = lambda n_, s_, d_: ffn.enter_context(nc.sbuf_tensor(n_, s_, d_))
    Hb0 = sF("Hb0", [128, HC * TP], BF16)
    Hb1 = sF("Hb1", [128, HC * TP], BF16)
    yTs = sF("yTs", [128, DC * TP], BF16)
    wblk = [sF(f"wblk{i}", [128, HC * 128], BF16) for i in range(2)]
    r2 = sF("r2", [128, 4 * D], F32)
    yft = [sF(f"yft{i}", [128, D], F32) for i in range(2)]
    sqd2 = sF("sqd2", [128, D], F32)
    st2 = sF("st2", [128, 32], F32)
    ffps = ExitStack()
    pF = [ffps.enter_context(nc.psum_tensor(f"pF{i}", [128, 512], F32)) for i in range(8)]

    pe0 = n["pe"]
    dv0 = n["dv"]
    ac0 = n["ac2"]
    STG = [("w1b", w1b, DC), ("w2b", w2b, DC), ("h1b", h1b, HC), ("h2b", h2b, HC)]

    def peval(p, bi):
        return pe0 + p * 192 + bi + 1

    def ac2v(p, kind, m):
        base = ac0 + p * 194
        if kind == 3:
            return base + 192 + m + 1
        return base + kind * 64 + m + 1

    def dvv(p, kind, m):
        base = dv0 + p * 83
        if kind == 0:
            return base + m + 1
        if kind == 1:
            return base + 65
        sub, q4 = m // 4, m % 4
        if kind == 2:
            return base + 65 + sub * 9 + q4 + 1
        if kind == 3:
            return base + 65 + m * 9 + 5
        return base + 65 + sub * 9 + 5 + q4 + 1

    wld = {}
    yfl = {}
    outst = {}

    cast_total = n["dC"]

    @block.sync
    def _(sy):
        sy.wait_ge(H["dv"], att_dv_done)
        sy.wait_ge(H["dC"], cast_total)
        for yn_ in ("dY00", "dY01", "dY10", "dY11"):
            sy.wait_ge(H[yn_], 8 * 32)
        for p in range(NP):
            g0 = p * TP
            sq_ = p // 2
            sy.wait_ge(H["dY%d0" % sq_], 8 * 32)
            sy.wait_ge(H["dY%d1" % sq_], 8 * 32)
            if p > 0:
                sy.wait_ge(H["pe"], peval(p - 1, 63))
            for k in range(DC):
                ins = sy.dma_start(out=yTs[:, k * TP:(k + 1) * TP],
                                   in_=yb[g0:g0 + TP, k * 128:(k + 1) * 128], transpose=True)
                inc("dYT", ins, 16)
            marks[("yT", p)] = n["dYT"]
            for si, (tag, wsrc, kk) in enumerate(STG):
                for c in range(HC):
                    bi = si * 32 + c
                    gbi = p * 192 + bi
                    pn = "dWB%d" % (bi % 2)
                    if gbi >= 2:
                        pp, pbi = (gbi - 2) // 192, (gbi - 2) % 192
                        sy.wait_ge(H["pe"], peval(pp, pbi))
                    ins = sy.dma_start(out=wblk[c % 2][:, 0:kk * 128], in_=wsrc[c])
                    wld[(p, si, c)] = (pn, inc(pn, ins, 16))
            for sub in range(2):
                for k in range(HC):
                    bi = 128 + sub * 32 + k
                    gbi = p * 192 + bi
                    pn = "dWB%d" % (bi % 2)
                    pp, pbi = (gbi - 2) // 192, (gbi - 2) % 192
                    sy.wait_ge(H["pe"], peval(pp, pbi))
                    ins = sy.dma_start(out=wblk[k % 2][:, 0:D], in_=w3b[k])
                    wld[("w3", p, sub, k)] = (pn, inc(pn, ins, 16))
                for q4 in range(4):
                    u = (p * 2 + sub) * 4 + q4
                    if u >= 2:
                        up, uu = (u - 2) // 8, (u - 2) % 8
                        sy.wait_ge(H["dv"], dvv(up, 2, uu))
                    r0 = g0 + sub * 512 + q4 * 128
                    pn = "dYF%d" % (q4 % 2)
                    ins = sy.dma_start(out=yft[q4 % 2][:, :], in_=yf[r0:r0 + 128, :])
                    yfl[(p, sub, q4)] = (pn, inc(pn, ins, 16))

    @block.tensor
    def _(t):
        for p in range(NP):
            for si, (tag, wsrc, kk) in enumerate(STG):
                if si == 0:
                    t.wait_ge(H["dYT"], marks[("yT", p)])
                if si == 2:
                    t.wait_ge(H["dv"], dvv(p, 1, 0))
                if si == 3:
                    t.wait_ge(H["ac2"], ac2v(p, 1, 63))
                rhs = yTs if si < 2 else (Hb0 if si == 2 else Hb1)
                for c in range(HC):
                    bi = si * 32 + c
                    wm = wld[(p, si, c)]
                    t.wait_ge(H[wm[0]], wm[1])
                    for half in range(2):
                        mg = si * 64 + c * 2 + half
                        gmg = p * 256 + mg
                        if gmg >= 4:
                            pp, pmg = (gmg - 4) // 256, (gmg - 4) % 256
                            psi, pm = pmg // 64, pmg % 64
                            if psi == 1:
                                t.wait_ge(H["dv"], dvv(pp, 0, pm))
                            else:
                                t.wait_ge(H["ac2"], ac2v(pp, 0 if psi == 0 else psi - 1, pm))
                        elif p > 0:
                            t.wait_ge(H["dv"], dvv(p - 1, 2, 4 + (mg % 4) // 2))
                        b = pF[mg % 4]
                        for k in range(kk):
                            ins = t.matmul(b[:, :], wblk[c % 2][:, k * 128:k * 128 + 128],
                                           rhs[:, k * TP + half * 512:k * TP + half * 512 + 512],
                                           start=(k == 0), stop=(k == kk - 1))
                    assert inc("pe", ins) == peval(p, bi)
            t.wait_ge(H["ac2"], ac2v(p, 2, 63))
            for sub in range(2):
                for k in range(HC):
                    bi = 128 + sub * 32 + k
                    wm = wld[("w3", p, sub, k)]
                    t.wait_ge(H[wm[0]], wm[1])
                    for q4 in range(4):
                        if k == 0:
                            if sub == 1:
                                t.wait_ge(H["dv"], dvv(p, 2, q4))
                            elif p > 0:
                                t.wait_ge(H["dv"], dvv(p - 1, 2, 4 + q4))
                        for half in range(2):
                            ins = t.matmul(pF[2 * q4 + half][:, :],
                                           Hb0[:, k * TP + sub * 512 + q4 * 128:k * TP + sub * 512 + q4 * 128 + 128],
                                           wblk[k % 2][:, half * 512:half * 512 + 512],
                                           start=(k == 0), stop=(k == HC - 1))
                    assert inc("pe", ins) == peval(p, bi)

    @block.gpsimd
    def _(g):
        for p in range(NP):
            for sub in range(2):
                for q4 in range(4):
                    g.wait_ge(H["dv"], dvv(p, 4, sub * 4 + q4))
                    sq = (p % 2) * 1024 + sub * 512 + q4 * 128
                    ins = g.dma_start(out=out[p // 2, sq:sq + 128, :], in_=r2[:, q4 * D:q4 * D + D])
                    inc("dO", ins, 16)
                outst[(p, sub)] = n["dO"]
        g.wait_ge(H["dO"], n["dO"])

    @block.scalar
    def _(sc):
        for p in range(NP):
            for si in (0, 2, 3):
                kind = 0 if si == 0 else si - 1
                dst = {0: Hb0, 2: Hb1, 3: Hb0}[si]
                for c in range(HC):
                    sc.wait_ge(H["pe"], peval(p, si * 32 + c))
                    for half in range(2):
                        mg = c * 2 + half
                        if si == 0:
                            sap = s12_sb[:, c:c + 1]
                            fn = AF.Silu
                        elif si == 2:
                            sap = sh12_sb[:, c:c + 1]
                            fn = AF.Gelu
                        else:
                            sap = sh12_sb[:, HC + c:HC + c + 1]
                            fn = AF.Gelu
                        ins = sc.activation(dst[:, c * TP + half * 512:c * TP + half * 512 + 512],
                                            pF[(si * 64 + mg) % 4][:, :], fn, scale=sap)
                        assert inc("ac2", ins) == ac2v(p, kind, mg)
            for sub in range(2):
                sc.wait_ge(H["dv"], dvv(p, 3, sub))
                aflush(sc, sc.activation(st2[:, 16:20], st2[:, 20:24], AF.Ln, bias=cst[:, 2:3]))
                ins = sc.activation(st2[:, 24:28], st2[:, 16:20], AF.Exp, scale=cst[:, 3:4])
                assert inc("ac2", ins) == ac2v(p, 3, sub)

    @block.vector
    def _(v):
        for p in range(NP):
            for c in range(HC):
                v.wait_ge(H["pe"], peval(p, 32 + c))
                for half in range(2):
                    mg = 64 + c * 2 + half
                    ins = v.tensor_single_scalar(
                        Hb1[:, c * TP + half * 512:c * TP + half * 512 + 512],
                        pF[mg % 4][:, :], s12_sb[:, HC + c:HC + c + 1], ALU.mult)
                    assert inc("dv", ins) == dvv(p, 0, c * 2 + half)
                v.wait_ge(H["ac2"], ac2v(p, 0, c * 2 + 1))
                v.wait_ge(H["dv"], dvv(p, 0, c * 2 + 1))
                ins = v.tensor_tensor(Hb0[:, c * TP:(c + 1) * TP], Hb0[:, c * TP:(c + 1) * TP],
                                      Hb1[:, c * TP:(c + 1) * TP], ALU.mult)
                if c == HC - 1:
                    assert inc("dv", ins) == dvv(p, 1, 0)
            for sub in range(2):
                v.wait_ge(H["pe"], peval(p, 128 + sub * 32 + 31))
                if p > 0 or sub > 0:
                    psub = (p * 2 + sub - 1)
                    v.wait_ge(H["dO"], outst[(psub // 2, psub % 2)])
                for q4 in range(4):
                    for half in range(2):
                        ins = v.tensor_tensor(r2[:, q4 * D + half * 512:q4 * D + half * 512 + 512],
                                              pF[2 * q4 + half][:, :],
                                              s3b_sb[:, half * 512:half * 512 + 512], ALU.mult)
                    vflush(v, ins)
                    ym = yfl[(p, sub, q4)]
                    v.wait_ge(H[ym[0]], ym[1])
                    v.scalar_tensor_tensor(r2[:, q4 * D:q4 * D + 512], r2[:, q4 * D:q4 * D + 512],
                                           1.0, yft[q4 % 2][:, 0:512], ALU.mult, ALU.add,
                                           accum_out=st2[:, q4:q4 + 1])
                    ins = v.scalar_tensor_tensor(r2[:, q4 * D + 512:q4 * D + 1024],
                                                 r2[:, q4 * D + 512:q4 * D + 1024],
                                                 1.0, yft[q4 % 2][:, 512:1024], ALU.mult, ALU.add,
                                                 accum_out=st2[:, 4 + q4:5 + q4])
                    assert inc("dv", ins) == dvv(p, 2, sub * 4 + q4)
                    v.wait_ge(H["dv"], n["dv"])
                    ins = v.scalar_tensor_tensor(sqd2[:, :], r2[:, q4 * D:q4 * D + D], 1.0,
                                                 r2[:, q4 * D:q4 * D + D], ALU.mult, ALU.mult,
                                                 accum_out=st2[:, 8 + q4:9 + q4])
                vflush(v, ins)
                vflush(v, v.tensor_tensor(st2[:, 12:16], st2[:, 0:4], st2[:, 4:8], ALU.add))
                vflush(v, v.tensor_single_scalar(st2[:, 12:16], st2[:, 12:16], 1.0 / D, ALU.mult))
                vflush(v, v.tensor_tensor(st2[:, 16:20], st2[:, 12:16], st2[:, 12:16], ALU.mult))
                vflush(v, v.tensor_single_scalar(st2[:, 20:24], st2[:, 8:12], 1.0 / D, ALU.mult))
                ins = v.tensor_tensor(st2[:, 20:24], st2[:, 20:24], st2[:, 16:20], ALU.subtract)
                assert inc("dv", ins) == dvv(p, 3, sub)
                v.wait_ge(H["ac2"], ac2v(p, 3, sub))
                vflush(v, v.tensor_tensor(st2[:, 28:32], st2[:, 12:16], st2[:, 24:28], ALU.mult))
                for q4 in range(4):
                    ins = v.tensor_scalar(r2[:, q4 * D:q4 * D + D], r2[:, q4 * D:q4 * D + D],
                                          st2[:, 24 + q4:25 + q4], st2[:, 28 + q4:29 + q4],
                                          ALU.mult, ALU.subtract)
                    assert inc("dv", ins) == dvv(p, 4, sub * 4 + q4)

    ffn.close()
    ffps.close()
    ctx.close()
    return nc


_NC_CACHE = {}


def kernel(L0_attn_x, L0_attn_wq_w, L0_attn_wq_b, L0_attn_wk_w, L0_attn_wk_b,
           L0_attn_wv_w, L0_attn_wv_b, L0_attn_wo_w, L0_attn_wo_b,
           L0_ffn_w1_w, L0_ffn_w1_s, L0_ffn_w2_w, L0_ffn_w2_s,
           L0_ffn_h1_w, L0_ffn_h1_s, L0_ffn_h2_w, L0_ffn_h2_s,
           L0_ffn_w3_w, L0_ffn_w3_s):
    f = np.float32
    x = np.asarray(L0_attn_x, f)

    def kslab(w):
        return np.ascontiguousarray(np.asarray(w, f).T.reshape(DC, 128, D).transpose(1, 0, 2))

    def cblk(w, kk):
        w = np.asarray(w, f)
        return np.ascontiguousarray(
            w.reshape(HC, 128, kk, 128).transpose(0, 3, 2, 1).reshape(HC, 128, kk * 128))

    def pcol(b):
        return np.ascontiguousarray(np.asarray(b, f).reshape(DC, 128).T)

    def brow(b):
        return np.ascontiguousarray(np.broadcast_to(np.asarray(b, f), (128, D)))

    def scol(sv):
        return np.ascontiguousarray(np.asarray(sv, f).reshape(HC, 128).T)

    shared = {
        "wq": kslab(L0_attn_wq_w), "wk": kslab(L0_attn_wk_w),
        "wv": kslab(L0_attn_wv_w), "wo": kslab(L0_attn_wo_w),
        "bqp": pcol(L0_attn_wq_b), "bkp": pcol(L0_attn_wk_b),
        "bvb": brow(L0_attn_wv_b), "bob": brow(L0_attn_wo_b),
        "w1": cblk(L0_ffn_w1_w, DC), "w2": cblk(L0_ffn_w2_w, DC),
        "h1": cblk(L0_ffn_h1_w, HC), "h2": cblk(L0_ffn_h2_w, HC),
        "w3": np.ascontiguousarray(np.asarray(L0_ffn_w3_w, f).T.reshape(HC, 128, D)),
        "s12": np.ascontiguousarray(np.concatenate([scol(L0_ffn_w1_s), scol(L0_ffn_w2_s)], axis=1)),
        "sh12": np.ascontiguousarray(np.concatenate([scol(L0_ffn_h1_s), scol(L0_ffn_h2_s)], axis=1)),
        "s3b": np.ascontiguousarray(np.broadcast_to(np.asarray(L0_ffn_w3_s, f), (128, D))),
        "idn": np.eye(128, dtype=f),
    }
    in_maps = []
    for c in range(NCORES):
        xs = x[SEQ * c:SEQ * (c + 1)]
        m = dict(shared)
        m["xr"] = np.ascontiguousarray(xs)
        m["xT"] = np.ascontiguousarray(xs.transpose(0, 2, 1).reshape(SEQ, DC, 128, S))
        in_maps.append(m)

    if "nc" not in _NC_CACHE:
        _NC_CACHE["nc"] = build()
    res = run_bass_kernel_spmd(_NC_CACHE["nc"], in_maps, core_ids=list(range(NCORES)))
    return np.concatenate([r["out"] for r in res.results], axis=0)
